# version 09
# baseline (speedup 1.0000x reference)
import numpy as np
import ml_dtypes
import concourse.bass as bass
import concourse.mybir as mybir
from concourse.bass_utils import run_bass_kernel_spmd
from concourse.bass_types import DynSlice

F32 = mybir.dt.float32
BF16 = mybir.dt.bfloat16
AF = mybir.ActivationFunctionType
ALU = mybir.AluOpType
AX = mybir.AxisListType
BFNP = ml_dtypes.bfloat16

B, S, D, HF = 16, 4096, 1024, 4096
NCORE = 8
P = 512
NT = 16
NTB = P // 128
NDC = D // 128
NHC = HF // 128
NU = 9
NH, DH, W = 16, 64, 128
EPS_LN, EPS_RMS = 1e-5, 1e-6
ATT_SCALE = 0.125
NEG = -1.0e30
STAGE = 3

D_PE_U, D_ACT_U, D_DVE_U, D_W1S_U, D_W2S_U = 40, 32, 8, 64, 64


def _masks():
    i = np.arange(W)[:, None]
    j = np.arange(2 * W)[None, :]
    base = (j > i) & (j <= i + W)
    first = base & (j >= W)
    ma = np.where(base, 0.0, NEG).astype(np.float32)
    mf = np.where(first, 0.0, NEG).astype(np.float32)
    return ma, mf


class KB:
    def __init__(self, nc):
        self.nc = nc
        self.eng = {"t": nc.tensor, "v": nc.vector, "s": nc.scalar,
                    "g": nc.gpsimd, "y": nc.sync}
        self.sems = {}
        self.regions = []

    def sem(self, name, waiters=()):
        st = {"sem": self.nc.alloc_semaphore(name=name), "count": 0,
              "ctrs": {}, "memo": {}}
        for el in waiters:
            e = self.eng[el]
            corr = e.alloc_register(f"{name}_{el}_c")
            tmp = e.alloc_register(f"{name}_{el}_t")
            e.reg_mov(corr, 0)
            st["ctrs"][el] = (corr, tmp)
        self.sems[name] = st
        return st

    def inc(self, instr, name, n):
        st = self.sems[name]
        instr.then_inc(st["sem"], n)
        st["count"] += n
        return st["count"]

    def cnt(self, name):
        return self.sems[name]["count"]

    def wait(self, el, name, token):
        st = self.sems[name]
        token = max(int(token), 0)
        memo = st["memo"].get(el)
        if memo is not None and memo >= token:
            return
        st["memo"][el] = token
        e = self.eng[el]
        if el in st["ctrs"]:
            corr, tmp = st["ctrs"][el]
            if token == 0:
                e.wait_ge(st["sem"], corr)
            else:
                e.reg_add(tmp, corr, token)
                e.wait_ge(st["sem"], tmp)
        else:
            e.wait_ge(st["sem"], token)

    def region_enter(self):
        self.regions.append({n: st["count"] for n, st in self.sems.items()})
        for st in self.sems.values():
            st["memo"].clear()

    def region_exit(self):
        entry = self.regions.pop()
        deltas = {}
        for n, st in self.sems.items():
            d = st["count"] - entry[n]
            deltas[n] = d
            if d:
                for el, (corr, tmp) in st["ctrs"].items():
                    self.eng[el].reg_add(corr, corr, d)
        return deltas

    def region_undo(self, deltas):
        for n, d in deltas.items():
            if d:
                st = self.sems[n]
                for el, (corr, tmp) in st["ctrs"].items():
                    self.eng[el].reg_sub(corr, corr, d)
        for st in self.sems.values():
            st["memo"].clear()


def build(stage=STAGE):
    nc = bass.Bass("TRN2", target_bir_lowering=False)
    kb = KB(nc)
    t, v, s, g, y = nc.tensor, nc.vector, nc.scalar, nc.gpsimd, nc.sync
    W_, I_ = kb.wait, kb.inc

    xin = nc.dram_tensor("xin", [NT * P, D], F32, kind="ExternalInput").ap()
    w1t = nc.dram_tensor("w1t", [NU, 8, 128, 4096], F32, kind="ExternalInput").ap()
    w2t = nc.dram_tensor("w2t", [NU, 4, 128, 8192], F32, kind="ExternalInput").ap()
    wqkv = nc.dram_tensor("wqkv", [6, 128, 4096], F32, kind="ExternalInput").ap()
    wo = nc.dram_tensor("wo", [2, 128, 4096], F32, kind="ExternalInput").ap()
    gwt = nc.dram_tensor("gwt", [D, 16], F32, kind="ExternalInput").ap()
    gbin = nc.dram_tensor("gbin", [128, D], F32, kind="ExternalInput").ap()
    maskt = nc.dram_tensor("maskt", [NT, 128, 2 * W], F32, kind="ExternalInput").ap()
    maska_d = nc.dram_tensor("maska_d", [128, 2 * W], F32, kind="ExternalInput").ap()
    xout = nc.dram_tensor("xout", [NT * P, D], F32, kind="ExternalOutput").ap()
    ident_d = nc.inline_tensor(np.eye(128, dtype=BFNP), name="ident").ap()

    w1b_d = nc.dram_tensor("w1b_d", [NU, 8, 128, 4096], BF16, kind="Internal").ap()
    w2b_d = nc.dram_tensor("w2b_d", [NU, 4, 128, 8192], BF16, kind="Internal").ap()
    wqkv_d = nc.dram_tensor("wqkv_d", [6, 128, 4096], BF16, kind="Internal").ap()
    wo_d = nc.dram_tensor("wo_d", [2, 128, 4096], BF16, kind="Internal").ap()
    gwt_d = nc.dram_tensor("gwt_d", [D, 16], BF16, kind="Internal").ap()

    xin_v = xin.rearrange("(ti tb p) d -> ti p tb d", p=128, tb=NTB)
    xout_v = xout.rearrange("(ti tb p) d -> ti p tb d", p=128, tb=NTB)

    A = nc.alloc_sbuf_tensor
    x_tm = A("x_tm", [128, NTB * D], F32)
    z_tm = A("z_tm", [128, NTB * D], BF16)
    zF = A("zF", [128, NDC * P], BF16)
    s_sb = A("s_sb", [128, NHC * P], BF16)
    w1buf = [A(f"w1buf{i}", [128, NDC * 512], BF16) for i in range(2)]
    w2buf = A("w2buf", [128, NHC * D], BF16)
    qF = A("qF", [128, 8 * P], BF16)
    kF = A("kF", [128, 8 * (P + W)], BF16)
    v_tm = A("v_tm", [128, (NTB + 1) * D], BF16)
    oF = A("oF", [128, 8 * P], BF16)
    sqs = A("sqs", [128, D], F32)
    ln2f = A("ln2f", [128, D], F32)
    stats = A("stats", [128, 64], F32)
    lg_sb = A("lg_sb", [128, NTB * 16], F32)
    gates = A("gates", [128, NTB * 16], F32)
    gsc = A("gsc", [128, 24], F32)
    gst = A("gst", [128, 8], F32)
    p32 = A("p32", [128, 4 * 2 * W], F32)
    pn_sb = A("pn_sb", [128, 4 * 2 * W], BF16)
    pT_sb = A("pT_sb", [128, 2 * 2 * W], BF16)
    srow = A("srow", [128, 16], F32)
    ident_sb = A("ident_sb", [128, 128], BF16)
    maska = A("maska_sb", [128, 2 * W], F32)
    mask0 = A("mask0_sb", [128, 2 * W], F32)
    g_b = A("g_b", [128, D], BF16)
    gwt_sb = A("gwt_sb", [128, NDC * 16], BF16)

    PA = nc.alloc_psum_tensor
    ps_h = [PA(f"ps_h{i}", [128, 512], F32) for i in range(2)]
    ps_o = [PA(f"ps_o{i}", [128, 512], F32) for i in range(2)]
    ps_t = [PA(f"ps_t{i}", [128, 128], BF16) for i in range(2)]
    ps_po = PA("ps_po", [128, 128], F32)
    ps_lg = PA("ps_lg", [128, 16], F32)

    kb.sem("pe", waiters=("t", "v", "s", "g", "y"))
    kb.sem("act", waiters=("t", "v", "s", "g"))
    kb.sem("dve", waiters=("t", "v", "s", "g"))
    kb.sem("w1s0", waiters=("t",))
    kb.sem("w1s1", waiters=("t",))
    kb.sem("w2s", waiters=("t",))
    kb.sem("xld", waiters=("v", "s"))
    kb.sem("mld", waiters=("v",))
    kb.sem("strs", waiters=("s",))
    kb.sem("halo", waiters=("t", "s"))
    kb.sem("cst")
    kb.sem("cld")

    for u in range(NU):
        I_(g.dma_start(out=w1b_d[u, :, :, :], in_=w1t[u, :, :, :]), "cst", 16)
        I_(g.dma_start(out=w2b_d[u, :, :, :], in_=w2t[u, :, :, :]), "cst", 16)
    I_(g.dma_start(out=gwt_d[:, :], in_=gwt[:, :]), "cst", 16)
    I_(g.dma_start(out=wqkv_d[:, :, :], in_=wqkv[:, :, :]), "cst", 16)
    I_(g.dma_start(out=wo_d[:, :, :], in_=wo[:, :, :]), "cst", 16)
    CAST_ALL = kb.cnt("cst")

    I_(y.dma_start(out=ident_sb[:], in_=ident_d[:, :]), "cld", 16)
    I_(y.dma_start(out=maska[:], in_=maska_d[:, :]), "cld", 16)
    I_(g.dma_start(out=g_b[:], in_=gbin[:, :]), "cld", 16)
    y.wait_ge(kb.sems["cst"]["sem"], CAST_ALL)
    s.wait_ge(kb.sems["cst"]["sem"], CAST_ALL)
    g.wait_ge(kb.sems["cst"]["sem"], CAST_ALL)
    I_(y.dma_start(out=gwt_sb.ap().rearrange("p (dc c) -> p dc c", c=16),
                   in_=gwt_d.rearrange("(dc p) c -> p dc c", p=128)), "cld", 16)
    CLD_ALL = kb.cnt("cld")
    g.sem_inc(kb.sems["strs"]["sem"], 16 * NTB)
    kb.sems["strs"]["count"] += 16 * NTB

    for fc in range(8):
        g.memset(kF[:, fc * (P + W) + P: (fc + 1) * (P + W)], 0.0)
    I_(g.memset(v_tm[:, NTB * D:(NTB + 1) * D], 0.0), "cld", 1)
    CLD_MEMS = kb.cnt("cld")

    st = {
        "pst": [("dve", 0), ("dve", 0)],
        "sqrt_act": {},
        "v1_dve": {},
        "xw": {},
        "po_free": [0, 0],
    }

    def ln_chain(tb, kind, x_dve_tok, x_act_tok, x_xld):
        c = tb * 8
        rs1 = stats[:, c + 0:c + 1]
        rs2 = stats[:, c + 1:c + 2]
        m_ = stats[:, c + 2:c + 3]
        negm = stats[:, c + 3:c + 4]
        v1 = stats[:, c + 4:c + 5]
        var = stats[:, c + 5:c + 6]
        stdv = stats[:, c + 6:c + 7]
        rr = stats[:, c + 7:c + 8]
        xb = x_tm[:, tb * D:(tb + 1) * D]

        if x_xld is not None:
            W_("s", "xld", x_xld)
            W_("v", "xld", x_xld)
        if x_dve_tok:
            W_("s", "dve", x_dve_tok)
            W_("v", "dve", x_dve_tok)
        if x_act_tok:
            W_("s", "act", x_act_tok)
        pv = st["v1_dve"].get(("A", tb))
        if pv:
            W_("s", "dve", pv)
        a1 = I_(s.activation(out=sqs[:], in_=xb, func=AF.Square, accum_out=rs2),
                "act", 1)
        d1 = I_(v.reduce_sum(out=rs1, in_=xb, axis=AX.X), "dve", 1)
        W_("v", "dve", d1)
        tm = I_(v.tensor_scalar_mul(m_, rs1, 1.0 / D), "dve", 1)
        W_("v", "act", a1)
        tv1 = I_(v.tensor_scalar(v1, rs2, 1.0 / D, EPS_LN, ALU.mult, ALU.add),
                 "dve", 1)
        st["v1_dve"][("A", tb)] = tv1
        W_("v", "dve", tm)
        tnm = I_(v.tensor_scalar_mul(negm, m_, -1.0), "dve", 1)
        W_("v", "dve", tnm)
        psq = st["sqrt_act"].get(tb)
        if psq:
            W_("v", "act", psq)
        tvar = I_(v.scalar_tensor_tensor(var, m_, negm[:, 0:1], v1,
                                         ALU.mult, ALU.add), "dve", 1)
        W_("s", "dve", tvar)
        a2 = I_(s.activation(out=stdv, in_=var, func=AF.Sqrt), "act", 1)
        st["sqrt_act"][tb] = a2
        W_("v", "act", a2)
        trr = I_(v.reciprocal(rr, stdv), "dve", 1)
        W_("v", "dve", trr)
        if kind == "z":
            tz = I_(v.tensor_scalar(z_tm[:, tb * D:(tb + 1) * D], xb,
                                    negm[:, 0:1], rr[:, 0:1], ALU.add, ALU.mult),
                    "dve", 1)
            return {"z": tz}
        pl2 = st.get("ln2f_act")
        if pl2:
            W_("v", "act", pl2)
        tl2 = I_(v.tensor_scalar(ln2f[:], xb, negm[:, 0:1], rr[:, 0:1],
                                 ALU.add, ALU.mult), "dve", 1)
        W_("v", "dve", tl2)
        txa = I_(v.tensor_add(xb, xb, ln2f[:]), "dve", 1)
        st["xw"][(tb, 0)] = txa
        st["xw"][(tb, 1)] = txa
        W_("s", "dve", tl2)
        a3 = I_(s.activation(out=sqs[:], in_=ln2f[:], func=AF.Square,
                             accum_out=rs2), "act", 1)
        st["ln2f_act"] = a3
        W_("v", "act", a3)
        tv2 = I_(v.tensor_scalar(v1, rs2, 1.0 / D, EPS_RMS, ALU.mult, ALU.add),
                 "dve", 1)
        st["v1_dve"][("A", tb)] = tv2
        W_("s", "dve", tv2)
        a4 = I_(s.activation(out=stdv, in_=v1, func=AF.Sqrt), "act", 1)
        st["sqrt_act"][tb] = a4
        W_("v", "act", a4)
        tr2 = I_(v.reciprocal(rr, stdv), "dve", 1)
        W_("v", "dve", tr2)
        W_("v", "cld", CLD_ALL)
        tz = I_(v.scalar_tensor_tensor(z_tm[:, tb * D:(tb + 1) * D], ln2f[:],
                                       rr[:, 0:1], g_b[:], ALU.mult, ALU.mult),
                "dve", 1)
        return {"z": tz, "xa": txa}

    def _sl(tb):
        c = tb * 8
        e = 32 + tb * 8
        return dict(
            rs1=stats[:, c + 0:c + 1], rs2=stats[:, c + 1:c + 2],
            m=stats[:, c + 2:c + 3], negm=stats[:, c + 3:c + 4],
            v1=stats[:, c + 4:c + 5], var=stats[:, c + 5:c + 6],
            stdv=stats[:, c + 6:c + 7], rr=stats[:, c + 7:c + 8],
            rr2=stats[:, e + 0:e + 1], ms=stats[:, e + 1:e + 2],
            ms2=stats[:, e + 2:e + 3], srms=stats[:, e + 3:e + 4],
            rrms=stats[:, e + 4:e + 5], s2=stats[:, e + 5:e + 6],
            c1=stats[:, e + 6:e + 7], c2=stats[:, e + 7:e + 8],
            xb=x_tm[:, tb * D:(tb + 1) * D])

    def ln4_stage1(tb, dep, a1t, d1t):
        q = _sl(tb)
        nm, tok = dep
        W_("s", nm, tok)
        W_("v", nm, tok)
        pv = st["v1_dve"].get(("A", tb))
        if pv:
            W_("s", "dve", pv)
        a1t[tb] = I_(s.activation(out=sqs[:], in_=q["xb"], func=AF.Square,
                                  accum_out=q["rs2"]), "act", 1)
        d1t[tb] = I_(v.reduce_sum(out=q["rs1"], in_=q["xb"], axis=AX.X),
                     "dve", 1)

    def ln4_rest(kind, a1t, d1t):
        tvar = {}
        for tb in range(NTB):
            q = _sl(tb)
            W_("v", "dve", d1t[tb])
            tm = I_(v.tensor_scalar_mul(q["m"], q["rs1"], 1.0 / D), "dve", 1)
            W_("v", "act", a1t[tb])
            tv1 = I_(v.tensor_scalar(q["v1"], q["rs2"], 1.0 / D, EPS_LN,
                                     ALU.mult, ALU.add), "dve", 1)
            st["v1_dve"][("A", tb)] = tv1
            W_("v", "dve", tm)
            tnm = I_(v.tensor_scalar_mul(q["negm"], q["m"], -1.0), "dve", 1)
            W_("v", "dve", tnm)
            psq = st["sqrt_act"].get(tb)
            if psq:
                W_("v", "act", psq)
            tvar[tb] = I_(v.scalar_tensor_tensor(q["var"], q["m"],
                                                 q["negm"][:, 0:1], q["v1"],
                                                 ALU.mult, ALU.add), "dve", 1)
        a2 = {}
        for tb in range(NTB):
            W_("s", "dve", tvar[tb])
            a2[tb] = I_(s.activation(out=_sl(tb)["stdv"], in_=_sl(tb)["var"],
                                     func=AF.Sqrt), "act", 1)
            st["sqrt_act"][tb] = a2[tb]
        trr = {}
        tms2 = {}
        for tb in range(NTB):
            q = _sl(tb)
            W_("v", "act", a2[tb])
            trr[tb] = I_(v.reciprocal(q["rr"], q["stdv"]), "dve", 1)
            if kind == "attn":
                W_("v", "dve", trr[tb])
                t_r2 = I_(v.tensor_mul(q["rr2"], q["rr"], q["rr"]), "dve", 1)
                W_("v", "dve", t_r2)
                t_ms = I_(v.scalar_tensor_tensor(q["ms"], q["var"], -EPS_LN,
                                                 q["rr2"], ALU.add, ALU.mult),
                          "dve", 1)
                W_("v", "dve", t_ms)
                tms2[tb] = I_(v.tensor_scalar(q["ms2"], q["ms"], 1.0, EPS_RMS,
                                              ALU.mult, ALU.add), "dve", 1)
        if kind == "z":
            z_toks = []
            for tb in range(NTB):
                q = _sl(tb)
                W_("v", "dve", trr[tb])
                tz = I_(v.tensor_scalar(z_tm[:, tb * D:(tb + 1) * D], q["xb"],
                                        q["negm"][:, 0:1], q["rr"][:, 0:1],
                                        ALU.add, ALU.mult), "dve", 1)
                z_toks.append(tz)
            return {"z": z_toks}
        a4 = {}
        for tb in range(NTB):
            W_("s", "dve", tms2[tb])
            a4[tb] = I_(s.activation(out=_sl(tb)["srms"], in_=_sl(tb)["ms2"],
                                     func=AF.Sqrt), "act", 1)
        tsc = {}
        for tb in range(NTB):
            q = _sl(tb)
            W_("v", "act", a4[tb])
            t_rm = I_(v.reciprocal(q["rrms"], q["srms"]), "dve", 1)
            W_("v", "dve", t_rm)
            t_s2 = I_(v.tensor_mul(q["s2"], q["rr"], q["rrms"]), "dve", 1)
            t_c1 = I_(v.tensor_scalar(q["c1"], q["rr"], 1.0, 1.0,
                                      ALU.mult, ALU.add), "dve", 1)
            W_("v", "dve", t_s2)
            tsc[tb] = I_(v.tensor_mul(q["c2"], q["negm"], q["rr"]), "dve", 1)
        z_toks = []
        t1t = {}
        tl_prev = 0
        for tb in range(NTB):
            q = _sl(tb)
            W_("v", "dve", tsc[tb])
            if tl_prev:
                W_("v", "dve", tl_prev)
            t1 = I_(v.tensor_scalar(ln2f[:], q["xb"], q["negm"][:, 0:1],
                                    q["s2"][:, 0:1], ALU.add, ALU.mult),
                    "dve", 1)
            t1t[tb] = t1
            W_("v", "dve", t1)
            W_("v", "cld", CLD_ALL)
            tz = I_(v.tensor_mul(z_tm[:, tb * D:(tb + 1) * D], ln2f[:],
                                 g_b[:]), "dve", 1)
            z_toks.append(tz)
            tl_prev = tz
        xa_toks = []
        for tb in range(NTB):
            q = _sl(tb)
            W_("v", "dve", t1t[tb])
            txa = I_(v.tensor_scalar(q["xb"], q["xb"], q["c1"][:, 0:1],
                                     q["c2"][:, 0:1], ALU.mult, ALU.add),
                     "dve", 1)
            st["xw"][(tb, 0)] = txa
            st["xw"][(tb, 1)] = txa
            xa_toks.append(txa)
        return {"z": z_toks, "xa": xa_toks}

    def emit_transposes(z_toks, zf_free_pe):
        W_("v", "pe", zf_free_pe)
        W_("t", "cld", 16)
        last = 0
        for tb in range(NTB):
            for dc in range(NDC):
                j = (tb * NDC + dc) % 2
                W_("t", "dve", z_toks[tb])
                W_("t", st["pst"][j][0], st["pst"][j][1])
                tt = I_(t.transpose(ps_t[j][:, :],
                                    z_tm[:, tb * D + dc * 128: tb * D + (dc + 1) * 128],
                                    ident_sb[:]), "pe", 1)
                W_("v", "pe", tt)
                tc = I_(v.tensor_copy(out=zF[:, dc * P + tb * 128: dc * P + (tb + 1) * 128],
                                      in_=ps_t[j][:, :]), "dve", 1)
                st["pst"][j] = ("dve", tc)
                last = tc
        return last

    def emit_gating(t_zf):
        prev_copy = 0
        for tb in range(NTB):
            W_("t", "dve", t_zf)
            W_("t", "dve", prev_copy)
            W_("t", "cld", CLD_ALL)
            for dc in range(NDC):
                mm = t.matmul(ps_lg[:, :],
                              zF[:, dc * P + tb * 128: dc * P + (tb + 1) * 128],
                              gwt_sb[:, dc * 16:(dc + 1) * 16],
                              start=(dc == 0), stop=(dc == NDC - 1))
            tpe = I_(mm, "pe", 1)
            W_("v", "pe", tpe)
            c = tb * 16
            lg8 = lg_sb[:, c: c + 8]
            tcp = I_(v.tensor_copy(out=lg_sb[:, c:c + 16], in_=ps_lg[:, :]), "dve", 1)
            prev_copy = tcp
            m1 = gst[:, 0:1]; m2 = gst[:, 1:2]; d12 = gst[:, 2:3]
            g1 = gst[:, 3:4]; g2h = gst[:, 4:5]
            eq1 = gsc[:, 0:8]; l2 = gsc[:, 8:16]; eq2 = gsc[:, 16:24]
            W_("v", "dve", tcp)
            tmz = I_(v.memset(gates[:, c:c + 1], 0.5), "dve", 1)
            t1 = I_(v.reduce_max(out=m1, in_=lg8, axis=AX.X), "dve", 1)
            W_("v", "dve", t1)
            t2 = I_(v.tensor_scalar(eq1, lg8, m1[:, 0:1], None, ALU.is_equal),
                    "dve", 1)
            W_("v", "dve", t2)
            t3 = I_(v.scalar_tensor_tensor(l2, eq1, NEG, lg8, ALU.mult, ALU.add),
                    "dve", 1)
            W_("v", "dve", t3)
            t4 = I_(v.reduce_max(out=m2, in_=l2, axis=AX.X), "dve", 1)
            W_("v", "dve", t4)
            t5 = I_(v.tensor_sub(d12, m1, m2), "dve", 1)
            W_("s", "dve", t5)
            a1 = I_(s.activation(out=g1, in_=d12, func=AF.Sigmoid), "act", 1)
            W_("v", "act", a1)
            t6 = I_(v.tensor_scalar(g2h, g1, -0.5, 0.5, ALU.mult, ALU.add),
                    "dve", 1)
            t7 = I_(v.tensor_scalar(eq2, l2, m2[:, 0:1], None, ALU.is_equal),
                    "dve", 1)
            W_("v", "dve", t7)
            t8 = I_(v.tensor_scalar(gates[:, c + 1:c + 9], eq1, g1[:, 0:1], 0.5,
                                    ALU.mult, ALU.mult), "dve", 1)
            W_("v", "dve", max(t6, t8))
            I_(v.scalar_tensor_tensor(gates[:, c + 1:c + 9], eq2, g2h[:, 0:1],
                                      gates[:, c + 1:c + 9], ALU.mult, ALU.add),
               "dve", 1)

    def emit_ffn_call(wsrc1, wsrc2, t_zf, qeng="y"):
        W_("t", "dve", t_zf)
        kb.region_enter()
        with nc.Fori(0, NU) as u:
            C0pe = kb.cnt("pe")
            C0a = kb.cnt("act")
            C0d = kb.cnt("dve")
            C0w1 = [kb.cnt("w1s0"), kb.cnt("w1s1")]
            C0w2 = kb.cnt("w2s")
            qe = kb.eng[qeng]
            for lc in range(8):
                bsel = lc % 2
                tok = (C0pe + 4 * (lc - 2) + 4 if lc >= 2
                       else C0pe + 4 * (lc + 6) + 4 - D_PE_U)
                W_(qeng, "pe", tok)
                dma = qe.dma_start(out=w1buf[bsel][:, :],
                                   in_=wsrc1[u, lc, :, :])
                I_(dma, f"w1s{bsel}", 16)
            ph1_toks = []
            for lc in range(8):
                bsel = lc % 2
                W_("t", f"w1s{bsel}", C0w1[bsel] + 16 * (lc // 2 + 1))
                for hh in range(4):
                    k = lc * 4 + hh
                    W_("t", "act", C0a + k - 1)
                    for dc in range(NDC):
                        mm = t.matmul(ps_h[k % 2][:, :],
                                      w1buf[bsel][:, dc * 512 + hh * 128: dc * 512 + (hh + 1) * 128],
                                      zF[:, dc * P:(dc + 1) * P],
                                      start=(dc == 0), stop=(dc == NDC - 1))
                    ph1_toks.append(I_(mm, "pe", 1))
            ph2_toks = []
            for tb in range(NTB):
                for half in range(2):
                    j = tb * 2 + half
                    if j > 0:
                        W_("t", "w2s", C0w2 + D_W2S_U)
                        W_("t", "act", C0a + NHC)
                    W_("t", "dve", C0d + j - 1)
                    for hc in range(NHC):
                        if j == 0 and hc % 8 == 0:
                            W_("t", "w2s", C0w2 + 16 * (hc // 8 + 1))
                        if j == 0 and hc in (0, 1, 24):
                            W_("t", "act", C0a + (1 if hc == 0 else
                                                  24 if hc == 1 else NHC))
                        mm = t.matmul(ps_o[j % 2][:, :],
                                      s_sb[:, hc * P + tb * 128: hc * P + (tb + 1) * 128],
                                      w2buf[:, hc * D + half * 512: hc * D + half * 512 + 512],
                                      start=(hc == 0), stop=(hc == NHC - 1))
                    ph2_toks.append(I_(mm, "pe", 1))
            for hc in range(NHC):
                W_("s", "pe", ph1_toks[hc])
                I_(s.activation(out=s_sb[:, hc * P:(hc + 1) * P],
                                in_=ps_h[hc % 2][:, :], func=AF.Silu), "act", 1)
            stt_toks = []
            for tb in range(NTB):
                for half in range(2):
                    j = tb * 2 + half
                    W_("v", "pe", ph2_toks[j])
                    W_("v", "dve", C0d + j - 7)
                    xbh = x_tm[:, tb * D + half * 512: tb * D + half * 512 + 512]
                    stt = v.scalar_tensor_tensor(
                        xbh, ps_o[j % 2][:, :],
                        gates[:, DynSlice(u + tb * 16, 1)],
                        xbh, ALU.mult, ALU.add)
                    stt_toks.append(I_(stt, "dve", 1))
            assert ph1_toks == [C0pe + i + 1 for i in range(32)]
            for j4 in range(4):
                W_("g", "pe", C0pe + 6 * j4)
                dma = g.dma_start(out=w2buf[:, j4 * 8 * D:(j4 + 1) * 8 * D],
                                  in_=wsrc2[u, j4, :, :])
                I_(dma, "w2s", 16)
            assert kb.cnt("pe") - C0pe == D_PE_U
            assert kb.cnt("act") - C0a == D_ACT_U
            assert kb.cnt("dve") - C0d == D_DVE_U
            assert kb.cnt("w1s0") - C0w1[0] == D_W1S_U
            assert kb.cnt("w1s1") - C0w1[1] == D_W1S_U
            assert kb.cnt("w2s") - C0w2 == D_W2S_U
            deltas = kb.region_exit()
        kb.region_undo(deltas)
        return stt_toks

    kb.region_enter()
    with nc.Fori(0, NT) as ti:
        PE_T0 = kb.cnt("pe")
        if stage >= 2:
            mdma = g.dma_start(out=mask0[:], in_=maskt[ti, :, :])
            t_mld = I_(mdma, "mld", 16)
        a1t, d1t = {}, {}
        for tb in range(NTB):
            W_("s", "strs", 16 * (tb + 1))
            xdma = s.dma_start(out=x_tm[:, tb * D:(tb + 1) * D],
                               in_=xin_v[ti, :, tb, :])
            t_xld = I_(xdma, "xld", 16)
            ln4_stage1(tb, ("xld", t_xld), a1t, d1t)
        z_toks = ln4_rest("z", a1t, d1t)["z"]
        if stage >= 2:
            W_("s", "pe", PE_T0)
            W_("s", "cld", CLD_MEMS)
            for fc in range(8):
                base = fc * (P + W)
                I_(s.dma_start(out=kF[:, base:base + W],
                               in_=kF[:, base + P:base + P + W]), "halo", 16)
            halo_tok = I_(s.dma_start(out=v_tm[:, 0:D],
                                      in_=v_tm[:, NTB * D:(NTB + 1) * D]),
                          "halo", 16)

        t_zf = emit_transposes(z_toks, PE_T0)
        emit_gating(t_zf)
        sttA = emit_ffn_call(w1b_d, w2b_d, t_zf)

        if stage >= 2:
            a1t, d1t = {}, {}
            for tb in range(NTB):
                ln4_stage1(tb, ("dve", sttA[2 * tb + 1]), a1t, d1t)
            r = ln4_rest("attn", a1t, d1t)
            z_toks = r["z"]
            xa_toks = r["xa"]
            T_PE_PREQKV = kb.cnt("pe")
            t_xnf = emit_transposes(z_toks, T_PE_PREQKV)
            C0q = [kb.cnt("w1s0"), kb.cnt("w1s1")]
            qkv_group_toks = []
            evac_toks = []
            W_("t", "dve", t_xnf)
            W_("t", "act", kb.cnt("act"))
            for jg in range(6):
                bsel = jg % 2
                W_("t", f"w1s{bsel}", C0q[bsel] + 16 * (jg // 2 + 1))
                if jg < 4:
                    for sub in range(4):
                        gidx = jg * 4 + sub
                        bank = [ps_h[0], ps_h[1], ps_o[0], ps_o[1]][gidx % 4]
                        if len(evac_toks) >= 4:
                            W_("t", "act", evac_toks[-4])
                        for dc in range(NDC):
                            mm = t.matmul(bank[:, :],
                                          w1buf[bsel][:, dc * 512 + sub * 128: dc * 512 + (sub + 1) * 128],
                                          zF[:, dc * P:(dc + 1) * P],
                                          start=(dc == 0), stop=(dc == NDC - 1))
                        tpe = I_(mm, "pe", 1)
                        qkv_group_toks.append(tpe)
                        W_("s", "pe", tpe)
                        if jg < 2:
                            fc = jg * 4 + sub
                            dst = qF[:, fc * P:(fc + 1) * P]
                        else:
                            fc = (jg - 2) * 4 + sub
                            dst = kF[:, fc * (P + W) + W: fc * (P + W) + W + P]
                            W_("s", "halo", halo_tok)
                        evac_toks.append(I_(s.copy(out=dst, in_=bank[:, :]), "act", 1))
                else:
                    for tb in range(NTB):
                        gidx = 16 + (jg - 4) * 4 + tb
                        bank = [ps_h[0], ps_h[1], ps_o[0], ps_o[1]][gidx % 4]
                        if len(evac_toks) >= 4:
                            W_("t", "act", evac_toks[-4])
                        for dc in range(NDC):
                            mm = t.matmul(bank[:, :],
                                          zF[:, dc * P + tb * 128: dc * P + (tb + 1) * 128],
                                          w1buf[bsel][:, dc * 512:(dc + 1) * 512],
                                          start=(dc == 0), stop=(dc == NDC - 1))
                        tpe = I_(mm, "pe", 1)
                        qkv_group_toks.append(tpe)
                        W_("s", "pe", tpe)
                        W_("s", "halo", halo_tok)
                        dst = v_tm[:, (1 + tb) * D + (jg - 4) * 512: (1 + tb) * D + (jg - 4) * 512 + 512]
                        evac_toks.append(I_(s.copy(out=dst, in_=bank[:, :]), "act", 1))
            T_QKV_PE = qkv_group_toks[-1]
            T_QKV_ACT = evac_toks[-1]
            for jg in range(6):
                bsel = jg % 2
                W_("y", "pe", qkv_group_toks[4 * (jg - 2) + 3] if jg >= 2
                   else T_PE_PREQKV)
                dma = y.dma_start(out=w1buf[bsel][:, :], in_=wqkv_d[jg, :, :])
                I_(dma, f"w1s{bsel}", 16)
            assert kb.cnt("w1s0") - C0q[0] == 48
            assert kb.cnt("w1s1") - C0q[1] == 48

            W_("t", "act", T_QKV_ACT)
            W_("t", "halo", halo_tok)
            heads = [(tb, h) for tb in range(NTB) for h in range(NH)]
            psc_banks = [ps_h[0], ps_h[1], ps_o[0], ps_o[1]]
            sc_tok = {}
            ms_tok = {}
            ex_tok = {}
            pn_tok = {}
            tr_tok = {}
            ptc_tok = {}
            pv_tok = {}
            of_toks = {}

            def att_score(si):
                tb, h = heads[si]
                fc, poff = h // 2, 64 * (h % 2)
                psc = psc_banks[si % 4]
                if si >= 4:
                    W_("t", "dve", ms_tok[si - 4])
                qsl = qF[:, fc * P + tb * 128: fc * P + (tb + 1) * 128]
                ksl = kF[:, fc * (P + W) + tb * 128: fc * (P + W) + tb * 128 + 2 * W]
                mm = t.matmul(psc[:, 0:2 * W], qsl[poff:poff + 64, :],
                              ksl[poff:poff + 64, :], start=True, stop=True)
                sc_tok[si] = I_(mm, "pe", 1)

            def att_soft_a(si):
                tb, h = heads[si]
                b4 = si % 4
                psl = p32[:, b4 * 2 * W:(b4 + 1) * 2 * W]
                sc = srow[:, b4 * 4: b4 * 4 + 4]
                mop = mask0 if tb == 0 else maska
                W_("v", "pe", sc_tok[si])
                if si >= 4:
                    W_("v", "act", ex_tok[si - 4])
                if tb == 0:
                    W_("v", "mld", t_mld)
                ms_tok[si] = I_(v.scalar_tensor_tensor(
                    psl, psc_banks[si % 4][:, 0:2 * W], ATT_SCALE, mop[:],
                    ALU.mult, ALU.add), "dve", 1)
                W_("v", "dve", ms_tok[si])
                trm = I_(v.reduce_max(out=sc[:, 0:1], in_=psl, axis=AX.X,
                                      negate=True), "dve", 1)
                W_("s", "dve", trm)
                if si >= 4:
                    W_("s", "dve", pn_tok[si - 4])
                ex_tok[si] = I_(s.activation(out=psl, in_=psl, func=AF.Exp,
                                             bias=sc[:, 0:1],
                                             accum_out=sc[:, 1:2]), "act", 1)

            def att_soft_b(si):
                b4 = si % 4
                psl = p32[:, b4 * 2 * W:(b4 + 1) * 2 * W]
                pnl = pn_sb[:, b4 * 2 * W:(b4 + 1) * 2 * W]
                sc = srow[:, b4 * 4: b4 * 4 + 4]
                W_("v", "act", ex_tok[si])
                trc = I_(v.reciprocal(sc[:, 2:3], sc[:, 1:2]), "dve", 1)
                W_("v", "dve", trc)
                if si >= 4:
                    W_("v", "pe", tr_tok[si - 4])
                pn_tok[si] = I_(v.tensor_scalar_mul(pnl, psl, sc[:, 2:3]),
                                "dve", 1)

            def att_pv(si):
                tb, h = heads[si]
                fc, poff = h // 2, 64 * (h % 2)
                b4, b2 = si % 4, si % 2
                pnl = pn_sb[:, b4 * 2 * W:(b4 + 1) * 2 * W]
                ptl = pT_sb[:, b2 * 2 * W:(b2 + 1) * 2 * W]
                W_("t", "dve", pn_tok[si])
                cps = []
                for kb2 in range(2):
                    W_("t", st["pst"][kb2][0], st["pst"][kb2][1])
                    tt = I_(t.transpose(ps_t[kb2][:, :],
                                        pnl[:, kb2 * W:(kb2 + 1) * W],
                                        ident_sb[:]), "pe", 1)
                    W_("s", "pe", tt)
                    if si >= 2:
                        W_("s", "pe", pv_tok[si - 2])
                    tc = I_(s.copy(out=ptl[:, kb2 * W:(kb2 + 1) * W],
                                   in_=ps_t[kb2][:, :]), "act", 1)
                    st["pst"][kb2] = ("act", tc)
                    cps.append(tc)
                tr_tok[si] = tt
                ptc_tok[si] = cps[1]
                W_("t", "act", cps[1])
                if si % 2 == 0 and si >= 2:
                    W_("t", "dve", st["po_free"])
                for kb2 in range(2):
                    vsl = v_tm[:, (tb + kb2) * D + h * DH: (tb + kb2) * D + (h + 1) * DH]
                    mm = t.matmul(ps_po[poff:poff + 64, :], vsl,
                                  ptl[:, kb2 * W:(kb2 + 1) * W],
                                  start=(kb2 == 0), stop=(kb2 == 1))
                pv_tok[si] = I_(mm, "pe", 1)

            def att_of(si):
                if si % 2 != 1:
                    return
                tb, h = heads[si]
                fc = h // 2
                W_("v", "pe", pv_tok[si])
                tof = I_(v.tensor_copy(
                    out=oF[:, fc * P + tb * 128: fc * P + (tb + 1) * 128],
                    in_=ps_po[:, :]), "dve", 1)
                of_toks[(tb, fc)] = tof
                st["po_free"] = tof

            NHEADS = len(heads)
            for si in range(4):
                att_score(si)
            for si in range(NHEADS + 4):
                if 3 <= si < NHEADS + 3:
                    att_of(si - 3)
                if 2 <= si < NHEADS + 2:
                    att_pv(si - 2)
                if si < NHEADS:
                    att_soft_a(si)
                    if si + 4 < NHEADS:
                        att_score(si + 4)
                if 1 <= si <= NHEADS:
                    att_soft_b(si - 1)
            st["pst"][0] = ("act", ptc_tok[NHEADS - 1])
            st["pst"][1] = ("act", ptc_tok[NHEADS - 1])
            C0o = [kb.cnt("w1s0"), kb.cnt("w1s1")]
            wo_mm_toks = []
            wo_add_toks = []
            z_toks = []
            for tb in range(NTB):
                for half in range(2):
                    j = tb * 2 + half
                    bsel = half
                    W_("t", f"w1s{bsel}", C0o[bsel] + 16)
                    for fc in range(8):
                        W_("t", "dve", of_toks[(tb, fc)])
                    if j < 2:
                        W_("t", "dve", ms_tok[62 + j])
                    else:
                        W_("t", "dve", wo_add_toks[j - 2])
                    for fc in range(8):
                        mm = t.matmul(ps_o[j % 2][:, :],
                                      oF[:, fc * P + tb * 128: fc * P + (tb + 1) * 128],
                                      w1buf[bsel][:, fc * 512:(fc + 1) * 512],
                                      start=(fc == 0), stop=(fc == 7))
                    wo_mm_toks.append(I_(mm, "pe", 1))
                    W_("v", "pe", wo_mm_toks[j])
                    W_("v", "dve", xa_toks[tb])
                    xbh = x_tm[:, tb * D + half * 512: tb * D + half * 512 + 512]
                    ta = I_(v.tensor_add(xbh, xbh, ps_o[j % 2][:, :]), "dve", 1)
                    wo_add_toks.append(ta)
                    st["xw"][(tb, half)] = ta
                if stage >= 3:
                    r = ln_chain(tb, "z", wo_add_toks[tb * 2 + 1], 0, None)
                    z_toks.append(r["z"])
            for bsel in range(2):
                W_("y", "pe", T_QKV_PE)
                dma = y.dma_start(out=w1buf[bsel][:, :], in_=wo_d[bsel, :, :])
                I_(dma, f"w1s{bsel}", 16)
            assert kb.cnt("w1s0") - C0o[0] == 16
            assert kb.cnt("w1s1") - C0o[1] == 16

        if stage >= 3:
            t_zf2 = emit_transposes(z_toks, T_QKV_PE)
            emit_gating(t_zf2)
            sttC = emit_ffn_call(w1b_d, w2b_d, t_zf2, qeng="y")

        for tb in range(NTB):
            W_("g", "dve", sttC[2 * tb + 1] if stage >= 3
               else kb.cnt("dve"))
            sdma = g.dma_start(out=xout_v[ti, :, tb, :],
                               in_=x_tm[:, tb * D:(tb + 1) * D])
            I_(sdma, "strs", 16)
        deltas = kb.region_exit()
    kb.region_undo(deltas)

    g.wait_ge(kb.sems["strs"]["sem"], 16 * NTB + 16 * NTB * NT)
    return nc


_CACHE = {}


def kernel(L0_attn_x, L0_attn_g, L0_attn_Wqkv, L0_attn_Wo,
           L0_ffn_shared_w1, L0_ffn_shared_w2, L0_ffn_experts_w1,
           L0_ffn_experts_w2, L0_ffn_gate_w, _stage=STAGE):
    x = np.ascontiguousarray(np.asarray(L0_attn_x, dtype=np.float32))
    gvec = np.asarray(L0_attn_g, dtype=np.float32)
    wqkv = np.ascontiguousarray(np.asarray(L0_attn_Wqkv, dtype=np.float32))
    wo = np.ascontiguousarray(np.asarray(L0_attn_Wo, dtype=np.float32))
    sw1 = np.asarray(L0_ffn_shared_w1, dtype=np.float32)
    sw2 = np.asarray(L0_ffn_shared_w2, dtype=np.float32)
    ew1 = np.asarray(L0_ffn_experts_w1, dtype=np.float32)
    ew2 = np.asarray(L0_ffn_experts_w2, dtype=np.float32)
    gw = np.asarray(L0_ffn_gate_w, dtype=np.float32)

    w1T = np.concatenate([sw1.T[None], ew1.transpose(0, 2, 1)], axis=0)
    w2T = np.concatenate([sw2.T[None], ew2.transpose(0, 2, 1)], axis=0)
    w1t = np.ascontiguousarray(
        w1T.reshape(NU, 8, 128, 8, 512).transpose(0, 3, 2, 1, 4)
        .reshape(NU, 8, 128, 4096))
    w2t = np.ascontiguousarray(
        w2T.reshape(NU, 4, 8, 128, 1024).transpose(0, 1, 3, 2, 4)
        .reshape(NU, 4, 128, 8192))
    wqkv = np.ascontiguousarray(
        wqkv.reshape(8, 128, 6, 512).transpose(2, 1, 0, 3).reshape(6, 128, 4096))
    wo = np.ascontiguousarray(
        wo.reshape(8, 128, 2, 512).transpose(2, 1, 0, 3).reshape(2, 128, 4096))
    gwt = np.zeros((D, 16), np.float32)
    gwt[:, 0:8] = gw.T
    gb = np.ascontiguousarray(np.broadcast_to(gvec[None, :], (128, D)))
    ma, mf = _masks()
    maskt = np.ascontiguousarray(
        np.stack([mf if (i % (NT // 2)) == 0 else ma for i in range(NT)], 0))

    if _stage not in _CACHE:
        _CACHE[_stage] = build(_stage)
    nc = _CACHE[_stage]

    in_maps = []
    for c in range(NCORE):
        xc = np.ascontiguousarray(x[2 * c:2 * c + 2].reshape(NT * P, D))
        in_maps.append({"xin": xc, "w1t": w1t, "w2t": w2t, "wqkv": wqkv,
                        "wo": wo, "gwt": gwt, "gbin": gb, "maskt": maskt,
                        "maska_d": ma})
    res = run_bass_kernel_spmd(nc, in_maps, core_ids=list(range(NCORE)))
    out = np.concatenate([r["xout"].reshape(2, S, D) for r in res.results], 0)
    return out


# version 10 (final)
# speedup vs baseline: 1.0035x; 1.0035x over previous
import numpy as np
import ml_dtypes
import concourse.bass as bass
import concourse.mybir as mybir
from concourse.bass_utils import run_bass_kernel_spmd
from concourse.bass_types import DynSlice

F32 = mybir.dt.float32
BF16 = mybir.dt.bfloat16
AF = mybir.ActivationFunctionType
ALU = mybir.AluOpType
AX = mybir.AxisListType
BFNP = ml_dtypes.bfloat16

B, S, D, HF = 16, 4096, 1024, 4096
NCORE = 8
P = 512
NT = 16
NTB = P // 128
NDC = D // 128
NHC = HF // 128
NU = 9
NH, DH, W = 16, 64, 128
EPS_LN, EPS_RMS = 1e-5, 1e-6
ATT_SCALE = 0.125
NEG = -1.0e30
STAGE = 3

D_PE_U, D_ACT_U, D_DVE_U, D_W1S_U, D_W2S_U = 40, 32, 8, 64, 64


def _masks():
    i = np.arange(W)[:, None]
    j = np.arange(2 * W)[None, :]
    base = (j > i) & (j <= i + W)
    first = base & (j >= W)
    ma = np.where(base, 0.0, NEG).astype(np.float32)
    mf = np.where(first, 0.0, NEG).astype(np.float32)
    return ma, mf


class KB:
    def __init__(self, nc):
        self.nc = nc
        self.eng = {"t": nc.tensor, "v": nc.vector, "s": nc.scalar,
                    "g": nc.gpsimd, "y": nc.sync}
        self.sems = {}
        self.regions = []

    def sem(self, name, waiters=()):
        st = {"sem": self.nc.alloc_semaphore(name=name), "count": 0,
              "ctrs": {}, "memo": {}}
        for el in waiters:
            e = self.eng[el]
            corr = e.alloc_register(f"{name}_{el}_c")
            tmp = e.alloc_register(f"{name}_{el}_t")
            e.reg_mov(corr, 0)
            st["ctrs"][el] = (corr, tmp)
        self.sems[name] = st
        return st

    def inc(self, instr, name, n):
        st = self.sems[name]
        instr.then_inc(st["sem"], n)
        st["count"] += n
        return st["count"]

    def cnt(self, name):
        return self.sems[name]["count"]

    def wait(self, el, name, token):
        st = self.sems[name]
        token = max(int(token), 0)
        memo = st["memo"].get(el)
        if memo is not None and memo >= token:
            return
        st["memo"][el] = token
        e = self.eng[el]
        if el in st["ctrs"]:
            corr, tmp = st["ctrs"][el]
            if token == 0:
                e.wait_ge(st["sem"], corr)
            else:
                e.reg_add(tmp, corr, token)
                e.wait_ge(st["sem"], tmp)
        else:
            e.wait_ge(st["sem"], token)

    def region_enter(self):
        self.regions.append({n: st["count"] for n, st in self.sems.items()})
        for st in self.sems.values():
            st["memo"].clear()

    def region_exit(self):
        entry = self.regions.pop()
        deltas = {}
        for n, st in self.sems.items():
            d = st["count"] - entry[n]
            deltas[n] = d
            if d:
                for el, (corr, tmp) in st["ctrs"].items():
                    self.eng[el].reg_add(corr, corr, d)
        return deltas

    def region_undo(self, deltas):
        for n, d in deltas.items():
            if d:
                st = self.sems[n]
                for el, (corr, tmp) in st["ctrs"].items():
                    self.eng[el].reg_sub(corr, corr, d)
        for st in self.sems.values():
            st["memo"].clear()


def build(stage=STAGE):
    nc = bass.Bass("TRN2", target_bir_lowering=False)
    kb = KB(nc)
    t, v, s, g, y = nc.tensor, nc.vector, nc.scalar, nc.gpsimd, nc.sync
    W_, I_ = kb.wait, kb.inc

    xin = nc.dram_tensor("xin", [NT * P, D], F32, kind="ExternalInput").ap()
    w1t = nc.dram_tensor("w1t", [NU, 8, 128, 4096], F32, kind="ExternalInput").ap()
    w2t = nc.dram_tensor("w2t", [NU, 4, 128, 8192], F32, kind="ExternalInput").ap()
    wqkv = nc.dram_tensor("wqkv", [6, 128, 4096], F32, kind="ExternalInput").ap()
    wo = nc.dram_tensor("wo", [2, 128, 4096], F32, kind="ExternalInput").ap()
    gwt = nc.dram_tensor("gwt", [D, 16], F32, kind="ExternalInput").ap()
    gbin = nc.dram_tensor("gbin", [128, D], F32, kind="ExternalInput").ap()
    maskt = nc.dram_tensor("maskt", [NT, 128, 2 * W], F32, kind="ExternalInput").ap()
    maska_d = nc.dram_tensor("maska_d", [128, 2 * W], F32, kind="ExternalInput").ap()
    xout = nc.dram_tensor("xout", [NT * P, D], F32, kind="ExternalOutput").ap()
    ident_d = nc.inline_tensor(np.eye(128, dtype=BFNP), name="ident").ap()

    w1b_d = nc.dram_tensor("w1b_d", [NU, 8, 128, 4096], BF16, kind="Internal").ap()
    w2b_d = nc.dram_tensor("w2b_d", [NU, 4, 128, 8192], BF16, kind="Internal").ap()
    wqkv_d = nc.dram_tensor("wqkv_d", [6, 128, 4096], BF16, kind="Internal").ap()
    wo_d = nc.dram_tensor("wo_d", [2, 128, 4096], BF16, kind="Internal").ap()
    gwt_d = nc.dram_tensor("gwt_d", [D, 16], BF16, kind="Internal").ap()

    xin_v = xin.rearrange("(ti tb p) d -> ti p tb d", p=128, tb=NTB)
    xout_v = xout.rearrange("(ti tb p) d -> ti p tb d", p=128, tb=NTB)

    A = nc.alloc_sbuf_tensor
    x_tm = A("x_tm", [128, NTB * D], F32)
    z_tm = A("z_tm", [128, NTB * D], BF16)
    zF = A("zF", [128, NDC * P], BF16)
    s_sb = A("s_sb", [128, NHC * P], BF16)
    w1buf = [A(f"w1buf{i}", [128, NDC * 512], BF16) for i in range(2)]
    w2buf = A("w2buf", [128, NHC * D], BF16)
    qF = A("qF", [128, 8 * P], BF16)
    kF = A("kF", [128, 8 * (P + W)], BF16)
    v_tm = A("v_tm", [128, (NTB + 1) * D], BF16)
    oF = A("oF", [128, 8 * P], BF16)
    sqs = A("sqs", [128, D], F32)
    ln2f = A("ln2f", [128, D], F32)
    stats = A("stats", [128, 64], F32)
    lg_sb = A("lg_sb", [128, NTB * 16], F32)
    gates = A("gates", [128, NTB * 16], F32)
    gsc = A("gsc", [128, 24], F32)
    gst = A("gst", [128, 8], F32)
    p32 = A("p32", [128, 4 * 2 * W], F32)
    pn_sb = A("pn_sb", [128, 4 * 2 * W], BF16)
    pT_sb = A("pT_sb", [128, 2 * 2 * W], BF16)
    srow = A("srow", [128, 16], F32)
    ident_sb = A("ident_sb", [128, 128], BF16)
    maska = A("maska_sb", [128, 2 * W], F32)
    mask0 = A("mask0_sb", [128, 2 * W], F32)
    g_b = A("g_b", [128, D], BF16)
    gwt_sb = A("gwt_sb", [128, NDC * 16], BF16)

    PA = nc.alloc_psum_tensor
    ps_h = [PA(f"ps_h{i}", [128, 512], F32) for i in range(2)]
    ps_o = [PA(f"ps_o{i}", [128, 512], F32) for i in range(2)]
    ps_t = [PA(f"ps_t{i}", [128, 128], BF16) for i in range(2)]
    ps_po = PA("ps_po", [128, 128], F32)
    ps_lg = PA("ps_lg", [128, 16], F32)

    kb.sem("pe", waiters=("t", "v", "s", "g", "y"))
    kb.sem("act", waiters=("t", "v", "s", "g"))
    kb.sem("dve", waiters=("t", "v", "s", "g"))
    kb.sem("w1s0", waiters=("t",))
    kb.sem("w1s1", waiters=("t",))
    kb.sem("w2s", waiters=("t",))
    kb.sem("xld", waiters=("v", "s"))
    kb.sem("mld", waiters=("v",))
    kb.sem("strs", waiters=("s",))
    kb.sem("halo", waiters=("t", "s"))
    kb.sem("cst")
    kb.sem("cld")

    for u in range(NU):
        I_(g.dma_start(out=w1b_d[u, :, :, :], in_=w1t[u, :, :, :]), "cst", 16)
        I_(g.dma_start(out=w2b_d[u, :, :, :], in_=w2t[u, :, :, :]), "cst", 16)
    I_(g.dma_start(out=gwt_d[:, :], in_=gwt[:, :]), "cst", 16)
    I_(g.dma_start(out=wqkv_d[:, :, :], in_=wqkv[:, :, :]), "cst", 16)
    I_(g.dma_start(out=wo_d[:, :, :], in_=wo[:, :, :]), "cst", 16)
    CAST_ALL = kb.cnt("cst")

    I_(y.dma_start(out=ident_sb[:], in_=ident_d[:, :]), "cld", 16)
    I_(y.dma_start(out=maska[:], in_=maska_d[:, :]), "cld", 16)
    I_(g.dma_start(out=g_b[:], in_=gbin[:, :]), "cld", 16)
    y.wait_ge(kb.sems["cst"]["sem"], CAST_ALL)
    s.wait_ge(kb.sems["cst"]["sem"], CAST_ALL)
    g.wait_ge(kb.sems["cst"]["sem"], CAST_ALL)
    I_(y.dma_start(out=gwt_sb.ap().rearrange("p (dc c) -> p dc c", c=16),
                   in_=gwt_d.rearrange("(dc p) c -> p dc c", p=128)), "cld", 16)
    CLD_ALL = kb.cnt("cld")
    g.sem_inc(kb.sems["strs"]["sem"], 16 * NTB)
    kb.sems["strs"]["count"] += 16 * NTB

    for fc in range(8):
        g.memset(kF[:, fc * (P + W) + P: (fc + 1) * (P + W)], 0.0)
    I_(g.memset(v_tm[:, NTB * D:(NTB + 1) * D], 0.0), "cld", 1)
    CLD_MEMS = kb.cnt("cld")

    st = {
        "pst": [("dve", 0), ("dve", 0)],
        "sqrt_act": {},
        "v1_dve": {},
        "xw": {},
        "po_free": [0, 0],
    }

    def ln_chain(tb, kind, x_dve_tok, x_act_tok, x_xld):
        c = tb * 8
        rs1 = stats[:, c + 0:c + 1]
        rs2 = stats[:, c + 1:c + 2]
        m_ = stats[:, c + 2:c + 3]
        negm = stats[:, c + 3:c + 4]
        v1 = stats[:, c + 4:c + 5]
        var = stats[:, c + 5:c + 6]
        stdv = stats[:, c + 6:c + 7]
        rr = stats[:, c + 7:c + 8]
        xb = x_tm[:, tb * D:(tb + 1) * D]

        if x_xld is not None:
            W_("s", "xld", x_xld)
            W_("v", "xld", x_xld)
        if x_dve_tok:
            W_("s", "dve", x_dve_tok)
            W_("v", "dve", x_dve_tok)
        if x_act_tok:
            W_("s", "act", x_act_tok)
        pv = st["v1_dve"].get(("A", tb))
        if pv:
            W_("s", "dve", pv)
        a1 = I_(s.activation(out=sqs[:], in_=xb, func=AF.Square, accum_out=rs2),
                "act", 1)
        d1 = I_(v.reduce_sum(out=rs1, in_=xb, axis=AX.X), "dve", 1)
        W_("v", "dve", d1)
        tm = I_(v.tensor_scalar_mul(m_, rs1, 1.0 / D), "dve", 1)
        W_("v", "act", a1)
        tv1 = I_(v.tensor_scalar(v1, rs2, 1.0 / D, EPS_LN, ALU.mult, ALU.add),
                 "dve", 1)
        st["v1_dve"][("A", tb)] = tv1
        W_("v", "dve", tm)
        tnm = I_(v.tensor_scalar_mul(negm, m_, -1.0), "dve", 1)
        W_("v", "dve", tnm)
        psq = st["sqrt_act"].get(tb)
        if psq:
            W_("v", "act", psq)
        tvar = I_(v.scalar_tensor_tensor(var, m_, negm[:, 0:1], v1,
                                         ALU.mult, ALU.add), "dve", 1)
        W_("s", "dve", tvar)
        a2 = I_(s.activation(out=stdv, in_=var, func=AF.Sqrt), "act", 1)
        st["sqrt_act"][tb] = a2
        W_("v", "act", a2)
        trr = I_(v.reciprocal(rr, stdv), "dve", 1)
        W_("v", "dve", trr)
        if kind == "z":
            tz = I_(v.tensor_scalar(z_tm[:, tb * D:(tb + 1) * D], xb,
                                    negm[:, 0:1], rr[:, 0:1], ALU.add, ALU.mult),
                    "dve", 1)
            return {"z": tz}
        pl2 = st.get("ln2f_act")
        if pl2:
            W_("v", "act", pl2)
        tl2 = I_(v.tensor_scalar(ln2f[:], xb, negm[:, 0:1], rr[:, 0:1],
                                 ALU.add, ALU.mult), "dve", 1)
        W_("v", "dve", tl2)
        txa = I_(v.tensor_add(xb, xb, ln2f[:]), "dve", 1)
        st["xw"][(tb, 0)] = txa
        st["xw"][(tb, 1)] = txa
        W_("s", "dve", tl2)
        a3 = I_(s.activation(out=sqs[:], in_=ln2f[:], func=AF.Square,
                             accum_out=rs2), "act", 1)
        st["ln2f_act"] = a3
        W_("v", "act", a3)
        tv2 = I_(v.tensor_scalar(v1, rs2, 1.0 / D, EPS_RMS, ALU.mult, ALU.add),
                 "dve", 1)
        st["v1_dve"][("A", tb)] = tv2
        W_("s", "dve", tv2)
        a4 = I_(s.activation(out=stdv, in_=v1, func=AF.Sqrt), "act", 1)
        st["sqrt_act"][tb] = a4
        W_("v", "act", a4)
        tr2 = I_(v.reciprocal(rr, stdv), "dve", 1)
        W_("v", "dve", tr2)
        W_("v", "cld", CLD_ALL)
        tz = I_(v.scalar_tensor_tensor(z_tm[:, tb * D:(tb + 1) * D], ln2f[:],
                                       rr[:, 0:1], g_b[:], ALU.mult, ALU.mult),
                "dve", 1)
        return {"z": tz, "xa": txa}

    def _sl(tb):
        c = tb * 8
        e = 32 + tb * 8
        return dict(
            rs1=stats[:, c + 0:c + 1], rs2=stats[:, c + 1:c + 2],
            m=stats[:, c + 2:c + 3], negm=stats[:, c + 3:c + 4],
            v1=stats[:, c + 4:c + 5], var=stats[:, c + 5:c + 6],
            stdv=stats[:, c + 6:c + 7], rr=stats[:, c + 7:c + 8],
            rr2=stats[:, e + 0:e + 1], ms=stats[:, e + 1:e + 2],
            ms2=stats[:, e + 2:e + 3], srms=stats[:, e + 3:e + 4],
            rrms=stats[:, e + 4:e + 5], s2=stats[:, e + 5:e + 6],
            c1=stats[:, e + 6:e + 7], c2=stats[:, e + 7:e + 8],
            xb=x_tm[:, tb * D:(tb + 1) * D])

    def ln4_stage1(tb, dep, a1t, d1t):
        q = _sl(tb)
        nm, tok = dep
        W_("s", nm, tok)
        W_("v", nm, tok)
        pv = st["v1_dve"].get(("A", tb))
        if pv:
            W_("s", "dve", pv)
        a1t[tb] = I_(s.activation(out=sqs[:], in_=q["xb"], func=AF.Square,
                                  accum_out=q["rs2"]), "act", 1)
        d1t[tb] = I_(v.reduce_sum(out=q["rs1"], in_=q["xb"], axis=AX.X),
                     "dve", 1)

    def ln4_rest(kind, a1t, d1t):
        tvar = {}
        for tb in range(NTB):
            q = _sl(tb)
            W_("v", "dve", d1t[tb])
            tm = I_(v.tensor_scalar_mul(q["m"], q["rs1"], 1.0 / D), "dve", 1)
            W_("v", "act", a1t[tb])
            tv1 = I_(v.tensor_scalar(q["v1"], q["rs2"], 1.0 / D, EPS_LN,
                                     ALU.mult, ALU.add), "dve", 1)
            st["v1_dve"][("A", tb)] = tv1
            W_("v", "dve", tm)
            tnm = I_(v.tensor_scalar_mul(q["negm"], q["m"], -1.0), "dve", 1)
            W_("v", "dve", tnm)
            psq = st["sqrt_act"].get(tb)
            if psq:
                W_("v", "act", psq)
            tvar[tb] = I_(v.scalar_tensor_tensor(q["var"], q["m"],
                                                 q["negm"][:, 0:1], q["v1"],
                                                 ALU.mult, ALU.add), "dve", 1)
        a2 = {}
        for tb in range(NTB):
            W_("s", "dve", tvar[tb])
            a2[tb] = I_(s.activation(out=_sl(tb)["stdv"], in_=_sl(tb)["var"],
                                     func=AF.Sqrt), "act", 1)
            st["sqrt_act"][tb] = a2[tb]
        trr = {}
        tms2 = {}
        for tb in range(NTB):
            q = _sl(tb)
            W_("v", "act", a2[tb])
            trr[tb] = I_(v.reciprocal(q["rr"], q["stdv"]), "dve", 1)
            if kind == "attn":
                W_("v", "dve", trr[tb])
                t_r2 = I_(v.tensor_mul(q["rr2"], q["rr"], q["rr"]), "dve", 1)
                W_("v", "dve", t_r2)
                t_ms = I_(v.scalar_tensor_tensor(q["ms"], q["var"], -EPS_LN,
                                                 q["rr2"], ALU.add, ALU.mult),
                          "dve", 1)
                W_("v", "dve", t_ms)
                tms2[tb] = I_(v.tensor_scalar(q["ms2"], q["ms"], 1.0, EPS_RMS,
                                              ALU.mult, ALU.add), "dve", 1)
        if kind == "z":
            z_toks = []
            for tb in range(NTB):
                q = _sl(tb)
                W_("v", "dve", trr[tb])
                tz = I_(v.tensor_scalar(z_tm[:, tb * D:(tb + 1) * D], q["xb"],
                                        q["negm"][:, 0:1], q["rr"][:, 0:1],
                                        ALU.add, ALU.mult), "dve", 1)
                z_toks.append(tz)
            return {"z": z_toks}
        a4 = {}
        for tb in range(NTB):
            W_("s", "dve", tms2[tb])
            a4[tb] = I_(s.activation(out=_sl(tb)["srms"], in_=_sl(tb)["ms2"],
                                     func=AF.Sqrt), "act", 1)
        tsc = {}
        for tb in range(NTB):
            q = _sl(tb)
            W_("v", "act", a4[tb])
            t_rm = I_(v.reciprocal(q["rrms"], q["srms"]), "dve", 1)
            W_("v", "dve", t_rm)
            t_s2 = I_(v.tensor_mul(q["s2"], q["rr"], q["rrms"]), "dve", 1)
            t_c1 = I_(v.tensor_scalar(q["c1"], q["rr"], 1.0, 1.0,
                                      ALU.mult, ALU.add), "dve", 1)
            W_("v", "dve", t_s2)
            tsc[tb] = I_(v.tensor_mul(q["c2"], q["negm"], q["rr"]), "dve", 1)
        z_toks = []
        t1t = {}
        tl_prev = 0
        for tb in range(NTB):
            q = _sl(tb)
            W_("v", "dve", tsc[tb])
            if tl_prev:
                W_("v", "dve", tl_prev)
            t1 = I_(v.tensor_scalar(ln2f[:], q["xb"], q["negm"][:, 0:1],
                                    q["s2"][:, 0:1], ALU.add, ALU.mult),
                    "dve", 1)
            t1t[tb] = t1
            W_("v", "dve", t1)
            W_("v", "cld", CLD_ALL)
            tz = I_(v.tensor_mul(z_tm[:, tb * D:(tb + 1) * D], ln2f[:],
                                 g_b[:]), "dve", 1)
            z_toks.append(tz)
            tl_prev = tz
        xa_toks = []
        for tb in range(NTB):
            q = _sl(tb)
            W_("v", "dve", t1t[tb])
            txa = I_(v.tensor_scalar(q["xb"], q["xb"], q["c1"][:, 0:1],
                                     q["c2"][:, 0:1], ALU.mult, ALU.add),
                     "dve", 1)
            st["xw"][(tb, 0)] = txa
            st["xw"][(tb, 1)] = txa
            xa_toks.append(txa)
        return {"z": z_toks, "xa": xa_toks}

    def emit_transposes(z_toks, zf_free_pe):
        W_("v", "pe", zf_free_pe)
        W_("t", "cld", 16)
        last = 0
        for tb in range(NTB):
            for dc in range(NDC):
                j = (tb * NDC + dc) % 2
                W_("t", "dve", z_toks[tb])
                W_("t", st["pst"][j][0], st["pst"][j][1])
                tt = I_(t.transpose(ps_t[j][:, :],
                                    z_tm[:, tb * D + dc * 128: tb * D + (dc + 1) * 128],
                                    ident_sb[:]), "pe", 1)
                W_("v", "pe", tt)
                tc = I_(v.tensor_copy(out=zF[:, dc * P + tb * 128: dc * P + (tb + 1) * 128],
                                      in_=ps_t[j][:, :]), "dve", 1)
                st["pst"][j] = ("dve", tc)
                last = tc
        return last

    def emit_gating(t_zf):
        prev_copy = 0
        for tb in range(NTB):
            W_("t", "dve", t_zf)
            W_("t", "dve", prev_copy)
            W_("t", "cld", CLD_ALL)
            for dc in range(NDC):
                mm = t.matmul(ps_lg[:, :],
                              zF[:, dc * P + tb * 128: dc * P + (tb + 1) * 128],
                              gwt_sb[:, dc * 16:(dc + 1) * 16],
                              start=(dc == 0), stop=(dc == NDC - 1))
            tpe = I_(mm, "pe", 1)
            W_("v", "pe", tpe)
            c = tb * 16
            lg8 = lg_sb[:, c: c + 8]
            tcp = I_(v.tensor_copy(out=lg_sb[:, c:c + 16], in_=ps_lg[:, :]), "dve", 1)
            prev_copy = tcp
            m1 = gst[:, 0:1]; m2 = gst[:, 1:2]; d12 = gst[:, 2:3]
            g1 = gst[:, 3:4]; g2h = gst[:, 4:5]
            eq1 = gsc[:, 0:8]; l2 = gsc[:, 8:16]; eq2 = gsc[:, 16:24]
            W_("v", "dve", tcp)
            tmz = I_(v.memset(gates[:, c:c + 1], 0.5), "dve", 1)
            t1 = I_(v.reduce_max(out=m1, in_=lg8, axis=AX.X), "dve", 1)
            W_("v", "dve", t1)
            t2 = I_(v.tensor_scalar(eq1, lg8, m1[:, 0:1], None, ALU.is_equal),
                    "dve", 1)
            W_("v", "dve", t2)
            t3 = I_(v.scalar_tensor_tensor(l2, eq1, NEG, lg8, ALU.mult, ALU.add),
                    "dve", 1)
            W_("v", "dve", t3)
            t4 = I_(v.reduce_max(out=m2, in_=l2, axis=AX.X), "dve", 1)
            W_("v", "dve", t4)
            t5 = I_(v.tensor_sub(d12, m1, m2), "dve", 1)
            W_("s", "dve", t5)
            a1 = I_(s.activation(out=g1, in_=d12, func=AF.Sigmoid), "act", 1)
            W_("v", "act", a1)
            t6 = I_(v.tensor_scalar(g2h, g1, -0.5, 0.5, ALU.mult, ALU.add),
                    "dve", 1)
            t7 = I_(v.tensor_scalar(eq2, l2, m2[:, 0:1], None, ALU.is_equal),
                    "dve", 1)
            W_("v", "dve", t7)
            t8 = I_(v.tensor_scalar(gates[:, c + 1:c + 9], eq1, g1[:, 0:1], 0.5,
                                    ALU.mult, ALU.mult), "dve", 1)
            W_("v", "dve", max(t6, t8))
            I_(v.scalar_tensor_tensor(gates[:, c + 1:c + 9], eq2, g2h[:, 0:1],
                                      gates[:, c + 1:c + 9], ALU.mult, ALU.add),
               "dve", 1)

    def emit_ffn_call(wsrc1, wsrc2, t_zf, qeng="y"):
        W_("t", "dve", t_zf)
        kb.region_enter()
        with nc.Fori(0, NU) as u:
            C0pe = kb.cnt("pe")
            C0a = kb.cnt("act")
            C0d = kb.cnt("dve")
            C0w1 = [kb.cnt("w1s0"), kb.cnt("w1s1")]
            C0w2 = kb.cnt("w2s")
            qe = kb.eng[qeng]
            for lc in range(8):
                bsel = lc % 2
                tok = (C0pe + 4 * (lc - 2) + 4 if lc >= 2
                       else C0pe + 4 * (lc + 6) + 4 - D_PE_U)
                W_(qeng, "pe", tok)
                dma = qe.dma_start(out=w1buf[bsel][:, :],
                                   in_=wsrc1[u, lc, :, :])
                I_(dma, f"w1s{bsel}", 16)
            ph1_toks = []
            for lc in range(8):
                bsel = lc % 2
                W_("t", f"w1s{bsel}", C0w1[bsel] + 16 * (lc // 2 + 1))
                for hh in range(4):
                    k = lc * 4 + hh
                    W_("t", "act", C0a + k - 1)
                    for dc in range(NDC):
                        mm = t.matmul(ps_h[k % 2][:, :],
                                      w1buf[bsel][:, dc * 512 + hh * 128: dc * 512 + (hh + 1) * 128],
                                      zF[:, dc * P:(dc + 1) * P],
                                      start=(dc == 0), stop=(dc == NDC - 1))
                    ph1_toks.append(I_(mm, "pe", 1))
            ph2_toks = []
            for tb in range(NTB):
                for half in range(2):
                    j = tb * 2 + half
                    if j > 0:
                        W_("t", "w2s", C0w2 + D_W2S_U)
                        W_("t", "act", C0a + NHC)
                    W_("t", "dve", C0d + j - 1)
                    hcs = (list(range(8, NHC)) + list(range(8))) if j == 0 \
                        else list(range(NHC))
                    for pos, hc in enumerate(hcs):
                        if j == 0:
                            if hc == 8:
                                W_("t", "w2s", C0w2 + 32)
                                W_("t", "act", C0a + 24)
                            elif hc == 16:
                                W_("t", "w2s", C0w2 + 48)
                            elif hc == 24:
                                W_("t", "w2s", C0w2 + 64)
                                W_("t", "act", C0a + NHC)
                        mm = t.matmul(ps_o[j % 2][:, :],
                                      s_sb[:, hc * P + tb * 128: hc * P + (tb + 1) * 128],
                                      w2buf[:, hc * D + half * 512: hc * D + half * 512 + 512],
                                      start=(pos == 0), stop=(pos == NHC - 1))
                    ph2_toks.append(I_(mm, "pe", 1))
            for hc in range(NHC):
                W_("s", "pe", ph1_toks[hc])
                I_(s.activation(out=s_sb[:, hc * P:(hc + 1) * P],
                                in_=ps_h[hc % 2][:, :], func=AF.Silu), "act", 1)
            stt_toks = []
            for tb in range(NTB):
                for half in range(2):
                    j = tb * 2 + half
                    W_("v", "pe", ph2_toks[j])
                    W_("v", "dve", C0d + j - 7)
                    xbh = x_tm[:, tb * D + half * 512: tb * D + half * 512 + 512]
                    stt = v.scalar_tensor_tensor(
                        xbh, ps_o[j % 2][:, :],
                        gates[:, DynSlice(u + tb * 16, 1)],
                        xbh, ALU.mult, ALU.add)
                    stt_toks.append(I_(stt, "dve", 1))
            assert ph1_toks == [C0pe + i + 1 for i in range(32)]
            for j4 in range(4):
                W_("g", "pe", C0pe + 6 * j4)
                dma = g.dma_start(out=w2buf[:, j4 * 8 * D:(j4 + 1) * 8 * D],
                                  in_=wsrc2[u, j4, :, :])
                I_(dma, "w2s", 16)
            assert kb.cnt("pe") - C0pe == D_PE_U
            assert kb.cnt("act") - C0a == D_ACT_U
            assert kb.cnt("dve") - C0d == D_DVE_U
            assert kb.cnt("w1s0") - C0w1[0] == D_W1S_U
            assert kb.cnt("w1s1") - C0w1[1] == D_W1S_U
            assert kb.cnt("w2s") - C0w2 == D_W2S_U
            deltas = kb.region_exit()
        kb.region_undo(deltas)
        return stt_toks

    kb.region_enter()
    with nc.Fori(0, NT) as ti:
        PE_T0 = kb.cnt("pe")
        if stage >= 2:
            mdma = g.dma_start(out=mask0[:], in_=maskt[ti, :, :])
            t_mld = I_(mdma, "mld", 16)
        a1t, d1t = {}, {}
        for tb in range(NTB):
            W_("s", "strs", 16 * (tb + 1))
            xdma = s.dma_start(out=x_tm[:, tb * D:(tb + 1) * D],
                               in_=xin_v[ti, :, tb, :])
            t_xld = I_(xdma, "xld", 16)
            ln4_stage1(tb, ("xld", t_xld), a1t, d1t)
        z_toks = ln4_rest("z", a1t, d1t)["z"]
        if stage >= 2:
            W_("s", "pe", PE_T0)
            W_("s", "cld", CLD_MEMS)
            for fc in range(8):
                base = fc * (P + W)
                I_(s.dma_start(out=kF[:, base:base + W],
                               in_=kF[:, base + P:base + P + W]), "halo", 16)
            halo_tok = I_(s.dma_start(out=v_tm[:, 0:D],
                                      in_=v_tm[:, NTB * D:(NTB + 1) * D]),
                          "halo", 16)

        t_zf = emit_transposes(z_toks, PE_T0)
        emit_gating(t_zf)
        sttA = emit_ffn_call(w1b_d, w2b_d, t_zf)

        if stage >= 2:
            a1t, d1t = {}, {}
            for tb in range(NTB):
                ln4_stage1(tb, ("dve", sttA[2 * tb + 1]), a1t, d1t)
            r = ln4_rest("attn", a1t, d1t)
            z_toks = r["z"]
            xa_toks = r["xa"]
            T_PE_PREQKV = kb.cnt("pe")
            t_xnf = emit_transposes(z_toks, T_PE_PREQKV)
            C0q = [kb.cnt("w1s0"), kb.cnt("w1s1")]
            qkv_group_toks = []
            evac_toks = []
            W_("t", "dve", t_xnf)
            W_("t", "act", kb.cnt("act"))
            for jg in range(6):
                bsel = jg % 2
                W_("t", f"w1s{bsel}", C0q[bsel] + 16 * (jg // 2 + 1))
                if jg < 4:
                    for sub in range(4):
                        gidx = jg * 4 + sub
                        bank = [ps_h[0], ps_h[1], ps_o[0], ps_o[1]][gidx % 4]
                        if len(evac_toks) >= 4:
                            W_("t", "act", evac_toks[-4])
                        for dc in range(NDC):
                            mm = t.matmul(bank[:, :],
                                          w1buf[bsel][:, dc * 512 + sub * 128: dc * 512 + (sub + 1) * 128],
                                          zF[:, dc * P:(dc + 1) * P],
                                          start=(dc == 0), stop=(dc == NDC - 1))
                        tpe = I_(mm, "pe", 1)
                        qkv_group_toks.append(tpe)
                        W_("s", "pe", tpe)
                        if jg < 2:
                            fc = jg * 4 + sub
                            dst = qF[:, fc * P:(fc + 1) * P]
                        else:
                            fc = (jg - 2) * 4 + sub
                            dst = kF[:, fc * (P + W) + W: fc * (P + W) + W + P]
                            W_("s", "halo", halo_tok)
                        evac_toks.append(I_(s.copy(out=dst, in_=bank[:, :]), "act", 1))
                else:
                    for tb in range(NTB):
                        gidx = 16 + (jg - 4) * 4 + tb
                        bank = [ps_h[0], ps_h[1], ps_o[0], ps_o[1]][gidx % 4]
                        if len(evac_toks) >= 4:
                            W_("t", "act", evac_toks[-4])
                        for dc in range(NDC):
                            mm = t.matmul(bank[:, :],
                                          zF[:, dc * P + tb * 128: dc * P + (tb + 1) * 128],
                                          w1buf[bsel][:, dc * 512:(dc + 1) * 512],
                                          start=(dc == 0), stop=(dc == NDC - 1))
                        tpe = I_(mm, "pe", 1)
                        qkv_group_toks.append(tpe)
                        W_("s", "pe", tpe)
                        W_("s", "halo", halo_tok)
                        dst = v_tm[:, (1 + tb) * D + (jg - 4) * 512: (1 + tb) * D + (jg - 4) * 512 + 512]
                        evac_toks.append(I_(s.copy(out=dst, in_=bank[:, :]), "act", 1))
            T_QKV_PE = qkv_group_toks[-1]
            T_QKV_ACT = evac_toks[-1]
            for jg in range(6):
                bsel = jg % 2
                W_("y", "pe", qkv_group_toks[4 * (jg - 2) + 3] if jg >= 2
                   else T_PE_PREQKV)
                dma = y.dma_start(out=w1buf[bsel][:, :], in_=wqkv_d[jg, :, :])
                I_(dma, f"w1s{bsel}", 16)
            assert kb.cnt("w1s0") - C0q[0] == 48
            assert kb.cnt("w1s1") - C0q[1] == 48

            W_("t", "act", T_QKV_ACT)
            W_("t", "halo", halo_tok)
            heads = [(tb, h) for tb in range(NTB) for h in range(NH)]
            psc_banks = [ps_h[0], ps_h[1], ps_o[0], ps_o[1]]
            sc_tok = {}
            ms_tok = {}
            ex_tok = {}
            pn_tok = {}
            tr_tok = {}
            ptc_tok = {}
            pv_tok = {}
            of_toks = {}

            def att_score(si):
                tb, h = heads[si]
                fc, poff = h // 2, 64 * (h % 2)
                psc = psc_banks[si % 4]
                if si >= 4:
                    W_("t", "dve", ms_tok[si - 4])
                qsl = qF[:, fc * P + tb * 128: fc * P + (tb + 1) * 128]
                ksl = kF[:, fc * (P + W) + tb * 128: fc * (P + W) + tb * 128 + 2 * W]
                mm = t.matmul(psc[:, 0:2 * W], qsl[poff:poff + 64, :],
                              ksl[poff:poff + 64, :], start=True, stop=True)
                sc_tok[si] = I_(mm, "pe", 1)

            def att_soft_a(si):
                tb, h = heads[si]
                b4 = si % 4
                psl = p32[:, b4 * 2 * W:(b4 + 1) * 2 * W]
                sc = srow[:, b4 * 4: b4 * 4 + 4]
                mop = mask0 if tb == 0 else maska
                W_("v", "pe", sc_tok[si])
                if si >= 4:
                    W_("v", "act", ex_tok[si - 4])
                if tb == 0:
                    W_("v", "mld", t_mld)
                ms_tok[si] = I_(v.scalar_tensor_tensor(
                    psl, psc_banks[si % 4][:, 0:2 * W], ATT_SCALE, mop[:],
                    ALU.mult, ALU.add), "dve", 1)
                W_("v", "dve", ms_tok[si])
                trm = I_(v.reduce_max(out=sc[:, 0:1], in_=psl, axis=AX.X,
                                      negate=True), "dve", 1)
                W_("s", "dve", trm)
                if si >= 4:
                    W_("s", "dve", pn_tok[si - 4])
                ex_tok[si] = I_(s.activation(out=psl, in_=psl, func=AF.Exp,
                                             bias=sc[:, 0:1],
                                             accum_out=sc[:, 1:2]), "act", 1)

            def att_soft_b(si):
                b4 = si % 4
                psl = p32[:, b4 * 2 * W:(b4 + 1) * 2 * W]
                pnl = pn_sb[:, b4 * 2 * W:(b4 + 1) * 2 * W]
                sc = srow[:, b4 * 4: b4 * 4 + 4]
                W_("v", "act", ex_tok[si])
                trc = I_(v.reciprocal(sc[:, 2:3], sc[:, 1:2]), "dve", 1)
                W_("v", "dve", trc)
                if si >= 4:
                    W_("v", "pe", tr_tok[si - 4])
                pn_tok[si] = I_(v.tensor_scalar_mul(pnl, psl, sc[:, 2:3]),
                                "dve", 1)

            def att_pv(si):
                tb, h = heads[si]
                fc, poff = h // 2, 64 * (h % 2)
                b4, b2 = si % 4, si % 2
                pnl = pn_sb[:, b4 * 2 * W:(b4 + 1) * 2 * W]
                ptl = pT_sb[:, b2 * 2 * W:(b2 + 1) * 2 * W]
                W_("t", "dve", pn_tok[si])
                cps = []
                for kb2 in range(2):
                    W_("t", st["pst"][kb2][0], st["pst"][kb2][1])
                    tt = I_(t.transpose(ps_t[kb2][:, :],
                                        pnl[:, kb2 * W:(kb2 + 1) * W],
                                        ident_sb[:]), "pe", 1)
                    W_("s", "pe", tt)
                    if si >= 2:
                        W_("s", "pe", pv_tok[si - 2])
                    tc = I_(s.copy(out=ptl[:, kb2 * W:(kb2 + 1) * W],
                                   in_=ps_t[kb2][:, :]), "act", 1)
                    st["pst"][kb2] = ("act", tc)
                    cps.append(tc)
                tr_tok[si] = tt
                ptc_tok[si] = cps[1]
                W_("t", "act", cps[1])
                if si % 2 == 0 and si >= 2:
                    W_("t", "dve", st["po_free"])
                for kb2 in range(2):
                    vsl = v_tm[:, (tb + kb2) * D + h * DH: (tb + kb2) * D + (h + 1) * DH]
                    mm = t.matmul(ps_po[poff:poff + 64, :], vsl,
                                  ptl[:, kb2 * W:(kb2 + 1) * W],
                                  start=(kb2 == 0), stop=(kb2 == 1))
                pv_tok[si] = I_(mm, "pe", 1)

            def att_of(si):
                if si % 2 != 1:
                    return
                tb, h = heads[si]
                fc = h // 2
                W_("v", "pe", pv_tok[si])
                tof = I_(v.tensor_copy(
                    out=oF[:, fc * P + tb * 128: fc * P + (tb + 1) * 128],
                    in_=ps_po[:, :]), "dve", 1)
                of_toks[(tb, fc)] = tof
                st["po_free"] = tof

            NHEADS = len(heads)
            for si in range(4):
                att_score(si)
            for si in range(NHEADS + 4):
                if 3 <= si < NHEADS + 3:
                    att_of(si - 3)
                if 2 <= si < NHEADS + 2:
                    att_pv(si - 2)
                if si < NHEADS:
                    att_soft_a(si)
                    if si + 4 < NHEADS:
                        att_score(si + 4)
                if 1 <= si <= NHEADS:
                    att_soft_b(si - 1)
            st["pst"][0] = ("act", ptc_tok[NHEADS - 1])
            st["pst"][1] = ("act", ptc_tok[NHEADS - 1])
            C0o = [kb.cnt("w1s0"), kb.cnt("w1s1")]
            wo_mm_toks = []
            wo_add_toks = []
            z_toks = []
            for tb in range(NTB):
                for half in range(2):
                    j = tb * 2 + half
                    bsel = half
                    W_("t", f"w1s{bsel}", C0o[bsel] + 16)
                    for fc in range(8):
                        W_("t", "dve", of_toks[(tb, fc)])
                    if j < 2:
                        W_("t", "dve", ms_tok[62 + j])
                    else:
                        W_("t", "dve", wo_add_toks[j - 2])
                    for fc in range(8):
                        mm = t.matmul(ps_o[j % 2][:, :],
                                      oF[:, fc * P + tb * 128: fc * P + (tb + 1) * 128],
                                      w1buf[bsel][:, fc * 512:(fc + 1) * 512],
                                      start=(fc == 0), stop=(fc == 7))
                    wo_mm_toks.append(I_(mm, "pe", 1))
                    W_("v", "pe", wo_mm_toks[j])
                    W_("v", "dve", xa_toks[tb])
                    xbh = x_tm[:, tb * D + half * 512: tb * D + half * 512 + 512]
                    ta = I_(v.tensor_add(xbh, xbh, ps_o[j % 2][:, :]), "dve", 1)
                    wo_add_toks.append(ta)
                    st["xw"][(tb, half)] = ta
                if stage >= 3:
                    r = ln_chain(tb, "z", wo_add_toks[tb * 2 + 1], 0, None)
                    z_toks.append(r["z"])
            for bsel in range(2):
                W_("y", "pe", T_QKV_PE)
                dma = y.dma_start(out=w1buf[bsel][:, :], in_=wo_d[bsel, :, :])
                I_(dma, f"w1s{bsel}", 16)
            assert kb.cnt("w1s0") - C0o[0] == 16
            assert kb.cnt("w1s1") - C0o[1] == 16

        if stage >= 3:
            t_zf2 = emit_transposes(z_toks, T_QKV_PE)
            emit_gating(t_zf2)
            sttC = emit_ffn_call(w1b_d, w2b_d, t_zf2, qeng="y")

        for tb in range(NTB):
            W_("g", "dve", sttC[2 * tb + 1] if stage >= 3
               else kb.cnt("dve"))
            sdma = g.dma_start(out=xout_v[ti, :, tb, :],
                               in_=x_tm[:, tb * D:(tb + 1) * D])
            I_(sdma, "strs", 16)
        deltas = kb.region_exit()
    kb.region_undo(deltas)

    g.wait_ge(kb.sems["strs"]["sem"], 16 * NTB + 16 * NTB * NT)
    return nc


_CACHE = {}


def kernel(L0_attn_x, L0_attn_g, L0_attn_Wqkv, L0_attn_Wo,
           L0_ffn_shared_w1, L0_ffn_shared_w2, L0_ffn_experts_w1,
           L0_ffn_experts_w2, L0_ffn_gate_w, _stage=STAGE):
    x = np.ascontiguousarray(np.asarray(L0_attn_x, dtype=np.float32))
    gvec = np.asarray(L0_attn_g, dtype=np.float32)
    wqkv = np.ascontiguousarray(np.asarray(L0_attn_Wqkv, dtype=np.float32))
    wo = np.ascontiguousarray(np.asarray(L0_attn_Wo, dtype=np.float32))
    sw1 = np.asarray(L0_ffn_shared_w1, dtype=np.float32)
    sw2 = np.asarray(L0_ffn_shared_w2, dtype=np.float32)
    ew1 = np.asarray(L0_ffn_experts_w1, dtype=np.float32)
    ew2 = np.asarray(L0_ffn_experts_w2, dtype=np.float32)
    gw = np.asarray(L0_ffn_gate_w, dtype=np.float32)

    w1T = np.concatenate([sw1.T[None], ew1.transpose(0, 2, 1)], axis=0)
    w2T = np.concatenate([sw2.T[None], ew2.transpose(0, 2, 1)], axis=0)
    w1t = np.ascontiguousarray(
        w1T.reshape(NU, 8, 128, 8, 512).transpose(0, 3, 2, 1, 4)
        .reshape(NU, 8, 128, 4096))
    w2t = np.ascontiguousarray(
        w2T.reshape(NU, 4, 8, 128, 1024).transpose(0, 1, 3, 2, 4)
        .reshape(NU, 4, 128, 8192))
    wqkv = np.ascontiguousarray(
        wqkv.reshape(8, 128, 6, 512).transpose(2, 1, 0, 3).reshape(6, 128, 4096))
    wo = np.ascontiguousarray(
        wo.reshape(8, 128, 2, 512).transpose(2, 1, 0, 3).reshape(2, 128, 4096))
    gwt = np.zeros((D, 16), np.float32)
    gwt[:, 0:8] = gw.T
    gb = np.ascontiguousarray(np.broadcast_to(gvec[None, :], (128, D)))
    ma, mf = _masks()
    maskt = np.ascontiguousarray(
        np.stack([mf if (i % (NT // 2)) == 0 else ma for i in range(NT)], 0))

    if _stage not in _CACHE:
        _CACHE[_stage] = build(_stage)
    nc = _CACHE[_stage]

    in_maps = []
    for c in range(NCORE):
        xc = np.ascontiguousarray(x[2 * c:2 * c + 2].reshape(NT * P, D))
        in_maps.append({"xin": xc, "w1t": w1t, "w2t": w2t, "wqkv": wqkv,
                        "wo": wo, "gwt": gwt, "gbin": gb, "maskt": maskt,
                        "maska_d": ma})
    res = run_bass_kernel_spmd(nc, in_maps, core_ids=list(range(NCORE)))
    out = np.concatenate([r["xout"].reshape(2, S, D) for r in res.results], 0)
    return out
